# version 03
# baseline (speedup 1.0000x reference)
import numpy as np
import ml_dtypes
from contextlib import ExitStack
import concourse.bass as bass
import concourse.mybir as mybir
from concourse.bass_utils import run_bass_kernel_spmd

F32 = mybir.dt.float32
BF16 = mybir.dt.bfloat16
AF = mybir.ActivationFunctionType
ALU = mybir.AluOpType
AX = mybir.AxisListType.X

H = 16
DM = 1024
S = 4096
NBLK = 8
NSUB = 4
EPS_LN = 1e-5
EPS_RMS = float(np.finfo(np.float32).eps)

ENGINES = ("tensor", "vector", "scalar", "sync", "gpsimd")
DMA_SEMS = ("w0", "w1", "w2", "w3", "w4", "w5", "w6",
            "xl0", "xl1", "scr", "yc", "axn", "ayl", "asl", "w8l")


class Prog:
    def __init__(self):
        self.q = {e: [] for e in ENGINES}
        self.cnt = {e: 0 for e in ENGINES}
        for s_ in DMA_SEMS:
            self.cnt[s_] = 0

    def op(self, eng, fn, wait=(), inc=None):
        if inc is not None:
            amt = 16 if inc in DMA_SEMS else 1
            self.cnt[inc] += amt
            val = self.cnt[inc]
        else:
            val = None
        self.q[eng].append((fn, tuple(wait), inc))
        return val


def build_nc(passes=("A", "B0", "B1", "B2", "C"), dbg=False, a_stage=99):
    nc = bass.Bass("TRN2", target_bir_lowering=False, detect_race_conditions=False)

    x_h = nc.dram_tensor("x", [S, DM], F32, kind="ExternalInput").ap()
    out_h = nc.dram_tensor("out", [S, DM], F32, kind="ExternalOutput").ap()
    wqT = nc.dram_tensor("wqT", [DM, DM], F32, kind="ExternalInput").ap()
    wkT = nc.dram_tensor("wkT", [DM, DM], F32, kind="ExternalInput").ap()
    wgT = nc.dram_tensor("wgT", [DM, DM], F32, kind="ExternalInput").ap()
    wvT = nc.dram_tensor("wvT", [DM, DM], F32, kind="ExternalInput").ap()
    wogT = nc.dram_tensor("wogT", [DM, DM], F32, kind="ExternalInput").ap()
    woT = nc.dram_tensor("woT", [DM, DM], F32, kind="ExternalInput").ap()
    wdT = nc.dram_tensor("wdT", [DM, H], F32, kind="ExternalInput").ap()
    wrT = nc.dram_tensor("wrT", [DM, 8], F32, kind="ExternalInput").ap()
    w1T = nc.dram_tensor("w1T", [8, DM, DM], F32, kind="ExternalInput").ap()
    w2T = nc.dram_tensor("w2T", [8, DM, DM], F32, kind="ExternalInput").ap()
    ws1T = nc.dram_tensor("ws1T", [DM, 2048], F32, kind="ExternalInput").ap()
    ws2T = nc.dram_tensor("ws2T", [2048, DM], F32, kind="ExternalInput").ap()
    bdr = nc.dram_tensor("bdr", [1, H], F32, kind="ExternalInput").ap()
    nwc = nc.dram_tensor("nwc", [128, 8], F32, kind="ExternalInput").ap()
    idbf_h = nc.dram_tensor("idbf", [128, 128], BF16, kind="ExternalInput").ap()
    idf_h = nc.dram_tensor("idf", [128, 128], F32, kind="ExternalInput").ap()
    i16_h = nc.dram_tensor("i16", [16, 16], F32, kind="ExternalInput").ap()
    ones16_h = nc.dram_tensor("ones16", [16, 128], F32, kind="ExternalInput").ap()
    ones1_h = nc.dram_tensor("ones1", [1, 128], F32, kind="ExternalInput").ap()
    mneg_h = nc.dram_tensor("mneg", [128, 128], F32, kind="ExternalInput").ap()
    exp1_h = nc.dram_tensor("exp1", [16, DM], BF16, kind="ExternalInput").ap()
    epsb_h = nc.dram_tensor("epsb", [128, 2], F32, kind="ExternalInput").ap()

    skind = "ExternalOutput" if dbg else "Internal"
    xnfm_scr = nc.dram_tensor("xnfm_scr", [8, 128, S], BF16, kind=skind).ap()
    w8_scr = nc.dram_tensor("w8_scr", [S, 8], F32, kind=skind).ap()
    s_scr = nc.dram_tensor("s_scr", [S, DM], BF16, kind=skind).ap()
    y_scr = [nc.dram_tensor(f"y{g}_scr", [S, DM], BF16, kind=skind).ap()
             for g in range(3)]
    ydbg = nc.dram_tensor("ydbg", [S, DM], BF16, kind=skind).ap() if dbg else None
    lndbg = nc.dram_tensor("lndbg", [NBLK * NSUB, 128, 8], F32, kind=skind).ap() if dbg else None
    rtdbg = nc.dram_tensor("rtdbg", [NBLK * NSUB, 128, 448], F32, kind=skind).ap() if dbg else None
    psdbg = nc.dram_tensor("psdbg", [NBLK, 128, 128], F32, kind=skind).ap() if dbg else None
    rt2 = nc.dram_tensor("rt2", [NBLK * NSUB, 128, 80], F32, kind=skind).ap() if dbg else None
    gdbg = nc.dram_tensor("gdbg", [NBLK, 128, 64], F32, kind=skind).ap() if dbg else None

    GROUPS = [(0, 1, 2), (3, 4, 5), (6, 7)]

    ctx = ExitStack()
    with ctx:
        wa = ctx.enter_context(nc.sbuf_tensor("wa", [128, 49152], BF16))
        ab = ctx.enter_context(nc.sbuf_tensor("ab", [128, 32768], BF16))
        af = ctx.enter_context(nc.sbuf_tensor("af", [128, 4096], F32))
        S32 = ctx.enter_context(nc.sbuf_tensor("S32", [128, 512], F32))
        Sbf = ctx.enter_context(nc.sbuf_tensor("Sbf", [128, 512], BF16))
        glog = ctx.enter_context(nc.sbuf_tensor("glog", [128, 64], F32))
        aT = ctx.enter_context(nc.sbuf_tensor("aT", [16, 128], F32))
        rT = ctx.enter_context(nc.sbuf_tensor("rT", [16, 128], F32))
        e2T = ctx.enter_context(nc.sbuf_tensor("e2T", [16, 128], BF16))
        lhsTt = ctx.enter_context(nc.sbuf_tensor("lhsTt", [16, 128], F32))
        aTM = ctx.enter_context(nc.sbuf_tensor("aTM", [128, 16], F32))
        expa = ctx.enter_context(nc.sbuf_tensor("expa", [128, 16], F32))
        ecol = ctx.enter_context(nc.sbuf_tensor("ecol", [128, 16], F32))
        bdbc = ctx.enter_context(nc.sbuf_tensor("bdbc", [128, 16], F32))
        dmp = ctx.enter_context(nc.sbuf_tensor("dmp", [128, 256], F32))
        gam = ctx.enter_context(nc.sbuf_tensor("gam", [128, 256], BF16))
        cols = ctx.enter_context(nc.sbuf_tensor("cols", [128, 1024], F32))
        ot1 = ctx.enter_context(nc.sbuf_tensor("ot1", [128, 128], F32))
        w8sb = ctx.enter_context(nc.sbuf_tensor("w8sb", [128, 64], F32))
        wd_bf = ctx.enter_context(nc.sbuf_tensor("wd_bf", [128, 128], BF16))
        wr_bf = ctx.enter_context(nc.sbuf_tensor("wr_bf", [128, 64], BF16))
        gin = ctx.enter_context(nc.sbuf_tensor("gin", [128, 1024], F32))
        sgb = ctx.enter_context(nc.sbuf_tensor("sgb", [128, 1024], F32))
        idbf = ctx.enter_context(nc.sbuf_tensor("idbf_s", [128, 128], BF16))
        idf = ctx.enter_context(nc.sbuf_tensor("idf_s", [128, 128], F32))
        i16 = ctx.enter_context(nc.sbuf_tensor("i16_s", [16, 16], F32))
        ones16 = ctx.enter_context(nc.sbuf_tensor("ones16_s", [16, 128], F32))
        ones1 = ctx.enter_context(nc.sbuf_tensor("ones1_s", [1, 128], F32))
        mneg = ctx.enter_context(nc.sbuf_tensor("mneg_s", [128, 128], F32))
        exp1 = ctx.enter_context(nc.sbuf_tensor("exp1_s", [16, DM], BF16))
        bd_s = ctx.enter_context(nc.sbuf_tensor("bd_s", [1, 16], F32))
        nw_s = ctx.enter_context(nc.sbuf_tensor("nw_s", [128, 8], F32))
        epsb = ctx.enter_context(nc.sbuf_tensor("epsb_s", [128, 2], F32))
        ps = [ctx.enter_context(nc.psum_tensor(f"ps{i}", [128, 512], F32))
              for i in range(8)]
        sems = {}
        for name in ENGINES + DMA_SEMS:
            sems[name] = ctx.enter_context(nc.semaphore(name))
        block = ctx.enter_context(nc.Block())

        P = Prog()
        op = P.op
        cnt = P.cnt

        XNFM, QFM, KGF, VTM, OGS, YTM = 0, 4096, 8192, 12288, 16384, 20480
        XNT, KT, E2X, SEV, GSP = 24576, 26624, 27648, 28672, 30720
        H2, Y0T, RPG = 4096, 8192, 12288
        H2C, Y2T, SBLK, RPGC = 4096, 12288, 16384, 20480

        def dma(out, in_):
            return lambda e: e.dma_start(out=out, in_=in_)

        for out_ap, in_ap in (
            (idbf[:], idbf_h), (idf[:], idf_h), (i16[:], i16_h),
            (ones16[:], ones16_h), (ones1[:], ones1_h), (mneg[:], mneg_h),
            (exp1[:], exp1_h), (bd_s[:], bdr), (nw_s[:], nwc), (epsb[:], epsb_h),
        ):
            op("sync", dma(out_ap, in_ap), inc="scr")
        consts_ld = cnt["scr"]

        WOFF = {"q": 0, "k": 8192, "g": 16384, "v": 24576, "og": 32768, "o": 40960}
        WSEM = {"q": "w0", "k": "w1", "g": "w2", "v": "w3", "og": "w4", "o": "w5"}
        wA = {}
        for name, src in (("q", wqT), ("k", wkT), ("g", wgT), ("v", wvT),
                          ("og", wogT), ("o", woT)):
            wA[name] = op("gpsimd", dma(
                wa[:, WOFF[name]:WOFF[name] + 8192].rearrange("p (k n) -> p k n", n=DM),
                src.rearrange("(k p) n -> p k n", p=128)), inc=WSEM[name])
        wA["d"] = op("gpsimd", dma(wd_bf[:].rearrange("p (k n) -> p k n", n=16),
                                   wdT.rearrange("(k p) n -> p k n", p=128)), inc="w6")
        wA["r"] = op("gpsimd", dma(wr_bf[:].rearrange("p (k n) -> p k n", n=8),
                                   wrT.rearrange("(k p) n -> p k n", p=128)), inc="w6")
        w6_all = cnt["w6"]

        wofold_ms = 0
        for k in range(8):
            wofold_ms = op("vector", lambda e, k=k: e.tensor_scalar_mul(
                wa[:, 40960 + k * DM:40960 + (k + 1) * DM],
                wa[:, 40960 + k * DM:40960 + (k + 1) * DM], nw_s[:, k:k + 1]),
                wait=((("w5", wA["o"]), ("scr", consts_ld)) if k == 0 else ()),
                inc="vector")

        bdmm_ms = op("tensor", lambda e: e.matmul(
            ps[5][:, 0:16], ones1[:], bd_s[:], start=True, stop=True),
            wait=(("scr", consts_ld),), inc="tensor")
        bdbc_ms = op("vector", lambda e: e.tensor_copy(bdbc[:], ps[5][:, 0:16]),
                     wait=(("tensor", bdmm_ms),), inc="vector")

        op("vector", lambda e: e.memset(cols[:], 0.0))
        s32_ready = op("vector", lambda e: e.memset(S32[:], 0.0), inc="vector")
        s32bf_ready = op("vector", lambda e: e.memset(Sbf[:], 0.0), inc="vector")

        if "A" in passes:
            NSTOT = NBLK * NSUB
            x_ms = [0] * NSTOT
            xslot_free = [[0, 0], [0, 0]]
            xnt_free = [0, 0]
            prev_o_evac_v = 0
            prev_gla_done_v = 0
            prev_gla_done_s = 0
            prev_inter15_t = 0
            last_gt3_t = 0
            last_ygf_v = 0
            last_yg_v = 0
            rms_sq_last_s = 0
            ps01_free = [[0, 0], [0, 0]]
            ps23_free = [0, 0]
            sev_ms = 0

            def emit_xload(idx):
                blk_, j_ = divmod(idx, NSUB)
                slot = idx % 2
                waits = []
                if xslot_free[slot][0]:
                    waits.append(("vector", xslot_free[slot][0]))
                if xslot_free[slot][1]:
                    waits.append(("scalar", xslot_free[slot][1]))
                x_ms[idx] = op("sync", dma(
                    af[:, slot * 1024:(slot + 1) * 1024],
                    x_h[blk_ * 512 + j_ * 128:blk_ * 512 + (j_ + 1) * 128, :]),
                    wait=tuple(waits), inc="xl0" if slot == 0 else "xl1")

            emit_xload(0)
            emit_xload(1)
            var_prev_v = 0
            lnd_prev = 0
            kg_prev = 0
            rt_prev = 0
            from collections import deque
            router_q = deque()
            router_lv = {}
            w8_store_ms = {}
            setfree = [[0, 0], [0, 0]]

            def pump():
                if router_q:
                    router_q.popleft()()

            for blk in range(NBLK):
                T0 = blk * 512
                if a_stage < 1:
                    continue
                ps6bf = ps[6][:].bitcast(BF16)
                xnfm_ready = 0
                xn_ready = [0] * NSUB
                for j in range(NSUB):
                    idx = blk * NSUB + j
                    slot = idx % 2
                    xsem = "xl0" if slot == 0 else "xl1"
                    xa = af[:, slot * 1024:(slot + 1) * 1024]
                    sqw2 = [(xsem, x_ms[idx])]
                    if var_prev_v:
                        sqw2.append(("vector", var_prev_v))
                    if kg_prev:
                        sqw2.append(("vector", kg_prev))
                    if dbg == "full" and lnd_prev:
                        sqw2.append(("scr", lnd_prev))
                    sq_ms = op("scalar", lambda e, xa=xa: e.activation(
                        ab[:, GSP + 1024:GSP + 2048], xa, AF.Square),
                       wait=tuple(sqw2), inc="scalar")
                    op("vector", lambda e, xa=xa: e.reduce_sum(
                        cols[:, 0:1], xa, axis=AX),
                       wait=((xsem, x_ms[idx]),))
                    op("vector", lambda e: e.reduce_sum(
                        cols[:, 1:2], ab[:, GSP + 1024:GSP + 2048], axis=AX),
                       wait=(("scalar", sq_ms),))
                    red_ms = op("vector", lambda e: e.reduce_sum(
                        cols[:, 78:79], cols[:, 0:2], axis=AX), inc="vector")
                    var_prev_v = red_ms
                    op("scalar", lambda e: e.mul(cols[:, 2:3], cols[:, 0:1], 1.0 / DM),
                       wait=(("vector", red_ms), ("scr", consts_ld)))
                    op("scalar", lambda e: e.activation(
                        cols[:, 3:4], cols[:, 2:3], AF.Square))
                    sm_ms = op("scalar", lambda e: e.mul(
                        cols[:, 4:5], cols[:, 1:2], 1.0 / DM), inc="scalar")
                    var_ms2 = op("vector", lambda e: e.tensor_tensor(
                        cols[:, 5:6], cols[:, 4:5], cols[:, 3:4], ALU.subtract),
                       wait=(("scalar", sm_ms),), inc="vector")
                    op("scalar", lambda e: e.activation(
                        cols[:, 6:7], cols[:, 5:6], AF.Ln, bias=epsb[:, 0:1]),
                       wait=(("vector", var_ms2),))
                    rstd_ms = op("scalar", lambda e: e.activation(
                        cols[:, 7:8], cols[:, 6:7], AF.Exp, scale=-0.5), inc="scalar")
                    xn_ready[j] = op("vector", lambda e, xa=xa, j=j: e.tensor_scalar(
                        ab[:, XNT + (j % 2) * 1024:XNT + (j % 2 + 1) * 1024],
                        xa, cols[:, 2:3], cols[:, 7:8], ALU.subtract, ALU.mult),
                       wait=(("scalar", rstd_ms), ("tensor", xnt_free[j % 2])),
                       inc="vector")
                    xslot_free[slot] = [xn_ready[j], sq_ms]
                    if dbg == "full":
                        lnd_prev = op("sync", dma(lndbg[idx, :, 0:7], cols[:, 0:7]),
                           wait=(("vector", xn_ready[j]), ("scalar", sq_ms)), inc="scr")
                    if idx + 2 < NSTOT:
                        emit_xload(idx + 2)
                    tw = [("vector", xn_ready[j])]
                    if j == 0 and last_ygf_v:
                        tw.append(("vector", last_ygf_v))
                    if j >= 1:
                        tw.append(("vector", xnfm_ready))
                    tl = 0
                    for c in range(8):
                        tl = op("tensor", lambda e, j=j, c=c: e.transpose(
                            ps6bf[:, c * 128:(c + 1) * 128],
                            ab[:, XNT + (j % 2) * 1024 + c * 128:XNT + (j % 2) * 1024 + (c + 1) * 128],
                            idbf[:]),
                            wait=tuple(tw) if c == 0 else (),
                            inc="tensor" if c == 7 else None)
                    xnt_free[j % 2] = tl
                    dst = ab[:, XNFM:XNFM + 4096].rearrange(
                        "p (c u) -> p c u", u=512)[:, :, j * 128:(j + 1) * 128]
                    src = ps6bf[:].rearrange("p (c u) -> p c u", u=128)
                    xnfm_ready = op("vector",
                                    lambda e, dst=dst, src=src: e.tensor_copy(dst, src),
                                    wait=(("tensor", tl),), inc="vector")
                xnfm_dma = op("sync", dma(
                    xnfm_scr[:, :, T0:T0 + 512].rearrange("c p u -> p c u"),
                    ab[:, XNFM:XNFM + 4096].rearrange("p (c u) -> p c u", u=512)),
                    wait=(("vector", xnfm_ready),), inc="scr")

                if a_stage < 2:
                    continue
                def pswait(i):
                    w = []
                    if ps01_free[i][0]:
                        w.append(("vector", ps01_free[i][0]))
                    if ps01_free[i][1]:
                        w.append(("scalar", ps01_free[i][1]))
                    return w

                grp = 0
                q_ready = kg_ready = v_ready = og_ready = 0
                for name, dsto in (("q", QFM), ("k", KGF)):
                    for c in range(8):
                        bank = grp % 2
                        for kk in range(8):
                            wv = []
                            if kk == 0:
                                wv = pswait(bank)
                                wv.append(("vector", xnfm_ready))
                                wv.append((WSEM[name], wA[name]))
                            mml = op("tensor", lambda e, bank=bank, woff=WOFF[name], c=c, kk=kk: e.matmul(
                                ps[bank][:],
                                wa[:, woff + kk * DM + c * 128:woff + kk * DM + (c + 1) * 128],
                                ab[:, XNFM + kk * 512:XNFM + (kk + 1) * 512],
                                start=(kk == 0), stop=(kk == 7)),
                                wait=tuple(wv), inc="tensor" if kk == 7 else None)
                        ev = op("vector", lambda e, bank=bank, dsto=dsto, c=c: e.tensor_copy(
                            ab[:, dsto + c * 512:dsto + (c + 1) * 512], ps[bank][:]),
                            wait=(("tensor", mml),), inc="vector")
                        ps01_free[bank][0] = ev
                        grp += 1
                        if c == 7:
                            if name == "q":
                                q_ready = ev
                if a_stage < 2.1:
                    continue
                for c in range(8):
                    bank = grp % 2
                    for kk in range(8):
                        wv = []
                        if kk == 0:
                            wv = pswait(bank)
                            wv.append(("w2", wA["g"]))
                        mml = op("tensor", lambda e, bank=bank, c=c, kk=kk: e.matmul(
                            ps[bank][:],
                            wa[:, WOFF["g"] + kk * DM + c * 128:WOFF["g"] + kk * DM + (c + 1) * 128],
                            ab[:, XNFM + kk * 512:XNFM + (kk + 1) * 512],
                            start=(kk == 0), stop=(kk == 7)),
                            wait=tuple(wv), inc="tensor" if kk == 7 else None)
                    gsl = GSP + (c % 4) * 512
                    sig_ms = op("scalar", lambda e, bank=bank, gsl=gsl: e.activation(
                        ab[:, gsl:gsl + 512], ps[bank][:], AF.Sigmoid),
                        wait=(("tensor", mml),), inc="scalar")
                    kg_ready = op("vector", lambda e, c=c, gsl=gsl: e.tensor_tensor(
                        ab[:, KGF + c * 512:KGF + (c + 1) * 512],
                        ab[:, KGF + c * 512:KGF + (c + 1) * 512],
                        ab[:, gsl:gsl + 512], ALU.mult),
                        wait=(("scalar", sig_ms),), inc="vector")
                    ps01_free[bank] = [kg_ready, sig_ms]
                    kg_prev = kg_ready
                    grp += 1
                if a_stage < 2.2:
                    continue
                for j in range(NSUB):
                    for nn_ in range(2):
                        bank = grp % 2
                        for kk in range(8):
                            wv = []
                            if kk == 0:
                                wv = pswait(bank)
                                wv.append(("w4", wA["og"]))
                                if j == 0 and nn_ == 0 and last_yg_v:
                                    wv.append(("vector", last_yg_v))
                            mml = op("tensor", lambda e, bank=bank, j=j, nn_=nn_, kk=kk: e.matmul(
                                ps[bank][:],
                                ab[:, XNFM + kk * 512 + j * 128:XNFM + kk * 512 + (j + 1) * 128],
                                wa[:, WOFF["og"] + kk * DM + nn_ * 512:WOFF["og"] + kk * DM + (nn_ + 1) * 512],
                                start=(kk == 0), stop=(kk == 7)),
                                wait=tuple(wv), inc="tensor" if kk == 7 else None)
                        og_ready = op("scalar", lambda e, bank=bank, j=j, nn_=nn_: e.activation(
                            ab[:, OGS + j * 1024 + nn_ * 512:OGS + j * 1024 + (nn_ + 1) * 512],
                            ps[bank][:], AF.Sigmoid),
                            wait=(("tensor", mml),), inc="scalar")
                        ps01_free[bank][1] = og_ready
                        grp += 1
                if a_stage < 2.3:
                    continue
                for j in range(NSUB):
                    for nn_ in range(2):
                        bank = grp % 2
                        for kk in range(8):
                            wv = []
                            if kk == 0:
                                wv = pswait(bank)
                                wv.append(("w3", wA["v"]))
                            mml = op("tensor", lambda e, bank=bank, j=j, nn_=nn_, kk=kk: e.matmul(
                                ps[bank][:],
                                ab[:, XNFM + kk * 512 + j * 128:XNFM + kk * 512 + (j + 1) * 128],
                                wa[:, WOFF["v"] + kk * DM + nn_ * 512:WOFF["v"] + kk * DM + (nn_ + 1) * 512],
                                start=(kk == 0), stop=(kk == 7)),
                                wait=tuple(wv), inc="tensor" if kk == 7 else None)
                        v_ready = op("vector", lambda e, bank=bank, j=j, nn_=nn_: e.tensor_copy(
                            ab[:, VTM + j * 1024 + nn_ * 512:VTM + j * 1024 + (nn_ + 1) * 512],
                            ps[bank][:]),
                            wait=(("tensor", mml),), inc="vector")
                        ps01_free[bank][0] = v_ready
                        grp += 1
                if a_stage < 2.4:
                    continue
                d_mm = [0] * NSUB
                r_mm = [0] * NSUB
                for j in range(NSUB):
                    for kk in range(8):
                        wv = []
                        if j == 0 and kk == 0:
                            if ps23_free[0]:
                                wv.append(("vector", ps23_free[0]))
                            if ps23_free[1]:
                                wv.append(("scalar", ps23_free[1]))
                        if kk == 0:
                            wv.append(("w6", w6_all))
                        op("tensor", lambda e, j=j, kk=kk: e.matmul(
                            ps[2][:, j * 16:(j + 1) * 16],
                            ab[:, XNFM + kk * 512 + j * 128:XNFM + kk * 512 + (j + 1) * 128],
                            wd_bf[:, kk * 16:(kk + 1) * 16],
                            start=(kk == 0), stop=False),
                            wait=tuple(wv))
                    d_mm[j] = op("tensor", lambda e, j=j: e.matmul(
                        ps[2][:, j * 16:(j + 1) * 16], ones1[:], bd_s[:],
                        start=False, stop=True), inc="tensor")
                for j in range(NSUB):
                    for kk in range(8):
                        rmw2 = []
                        if j == 0 and kk == 0:
                            if prev_gla_done_v:
                                rmw2.append(("vector", prev_gla_done_v))
                            if prev_gla_done_s:
                                rmw2.append(("scalar", prev_gla_done_s))
                        r_mm[j] = op("tensor", lambda e, j=j, kk=kk: e.matmul(
                            ps[3][:, 64 + j * 8:64 + (j + 1) * 8],
                            ab[:, XNFM + kk * 512 + j * 128:XNFM + kk * 512 + (j + 1) * 128],
                            wr_bf[:, kk * 8:(kk + 1) * 8],
                            start=(kk == 0), stop=(kk == 7)),
                            wait=tuple(rmw2), inc="tensor" if kk == 7 else None)
                if a_stage < 2.5:
                    continue
                if dbg == "full":
                    pscp = op("vector", lambda e: e.tensor_copy(
                        gin[:, 0:64], ps[2][:, 0:64]),
                       wait=(("tensor", d_mm[NSUB - 1]),), inc="vector")
                    op("sync", dma(psdbg[blk, :, 0:64], gin[:, 0:64]),
                       wait=(("vector", pscp),), inc="scr")
                glog_ready = [0] * NSUB
                l1_ms = 0
                for j in range(NSUB):
                    spw = [("tensor", d_mm[NSUB - 1]), ("vector", bdbc_ms)]
                    if dbg == "full" and rt_prev:
                        spw.append(("scr", rt_prev))
                    if j > 0:
                        spw.append(("vector", glog_ready[j - 1]))
                    if last_gt3_t:
                        spw.append(("tensor", last_gt3_t))
                    azc, e1c, l1c, mzc = (192 + j * 16, 256 + j * 16,
                                          320 + j * 16, 384 + j * 16)
                    zsl = ps[2][:, j * 16:(j + 1) * 16]
                    op("scalar", lambda e, zsl=zsl, azc=azc: e.activation(
                        cols[:, azc:azc + 16], zsl, AF.Abs),
                       wait=tuple(spw))
                    op("scalar", lambda e, azc=azc, e1c=e1c: e.activation(
                        cols[:, e1c:e1c + 16], cols[:, azc:azc + 16], AF.Exp,
                        scale=-1.0))
                    l1_ms = op("scalar", lambda e, e1c=e1c, l1c=l1c: e.activation(
                        cols[:, l1c:l1c + 16], cols[:, e1c:e1c + 16], AF.Ln,
                        bias=1.0), inc="scalar")
                    op("vector", lambda e, zsl=zsl, mzc=mzc: e.tensor_scalar(
                        cols[:, mzc:mzc + 16], zsl, -1.0, 0.0,
                        ALU.mult, ALU.min),
                       wait=tuple(spw))
                    glog_ready[j] = op("vector", lambda e, j=j, mzc=mzc, l1c=l1c: e.tensor_tensor(
                        glog[:, j * 16:(j + 1) * 16], cols[:, mzc:mzc + 16],
                        cols[:, l1c:l1c + 16], ALU.subtract),
                        wait=(("scalar", l1_ms),), inc="vector")
                if a_stage < 2.6:
                    continue
                def mk_router(blk=blk, r_mm_last=r_mm[NSUB - 1]):
                    b0 = 448 + (blk % 2) * 288
                    LV, LSB, MX, SEL = b0, b0 + 32, b0 + 64, b0 + 96
                    LSH, LSH2, W8E = b0 + 128, b0 + 160, b0 + 192
                    DM, ED, LDEN = b0 + 224, b0 + 228, b0 + 232
                    wslot = (blk % 2) * 32
                    st = {}

                    def r1():
                        w = [("tensor", r_mm_last)]
                        prev = setfree[blk % 2]
                        if prev[0]:
                            w.append(("vector", prev[0]))
                        if prev[1]:
                            w.append(("scalar", prev[1]))
                        st["lv"] = op("vector", lambda e: e.tensor_scalar_mul(
                            cols[:, LV:LV + 32], ps[3][:, 64:96], 1.0),
                            wait=tuple(w), inc="vector")
                        router_lv[blk] = st["lv"]

                    def r2():
                        st["lsb"] = op("scalar", lambda e: e.mul(
                            cols[:, LSB:LSB + 32], cols[:, LV:LV + 32], 1.0),
                            wait=(("vector", st["lv"]),), inc="scalar")

                    def r3():
                        for jj in range(4):
                            mm = op("vector", lambda e, jj=jj: e.max(
                                cols[:, MX + jj * 8:MX + (jj + 1) * 8],
                                cols[:, LSB + jj * 8:LSB + (jj + 1) * 8]),
                               wait=(("scalar", st["lsb"]),) if jj == 0 else (),
                               inc="vector" if jj == 3 else None)
                        st["mx"] = mm

                    def r4():
                        for jj in range(4):
                            op("vector", lambda e, jj=jj: e.tensor_tensor(
                                cols[:, SEL + jj * 8:SEL + (jj + 1) * 8],
                                cols[:, LSB + jj * 8:LSB + (jj + 1) * 8],
                                cols[:, MX + jj * 8 + 1:MX + jj * 8 + 2].broadcast_to([128, 8]),
                                ALU.is_ge),
                               wait=(("vector", st["mx"]),) if jj == 0 else ())
                        for jj in range(4):
                            op("vector", lambda e, jj=jj: e.tensor_tensor(
                                cols[:, LSH + jj * 8:LSH + (jj + 1) * 8],
                                cols[:, LSB + jj * 8:LSB + (jj + 1) * 8],
                                cols[:, MX + jj * 8:MX + jj * 8 + 1].broadcast_to([128, 8]),
                                ALU.subtract))
                        st["dm"] = op("vector", lambda e: e.tensor_sub(
                            cols[:, DM:DM + 4],
                            cols[:, MX:MX + 32].rearrange("p (j e) -> p j e", e=8)[:, :, 1:2].squeeze(2),
                            cols[:, MX:MX + 32].rearrange("p (j e) -> p j e", e=8)[:, :, 0:1].squeeze(2)),
                            inc="vector")

                    def r5():
                        op("scalar", lambda e: e.activation(
                            cols[:, ED:ED + 4], cols[:, DM:DM + 4], AF.Exp),
                           wait=(("vector", st["dm"]),))
                        st["lden"] = op("scalar", lambda e: e.activation(
                            cols[:, LDEN:LDEN + 4], cols[:, ED:ED + 4], AF.Ln,
                            bias=1.0), inc="scalar")

                    def r6():
                        for jj in range(4):
                            st["lsh2"] = op("vector", lambda e, jj=jj: e.tensor_tensor(
                                cols[:, LSH2 + jj * 8:LSH2 + (jj + 1) * 8],
                                cols[:, LSH + jj * 8:LSH + (jj + 1) * 8],
                                cols[:, LDEN + jj:LDEN + jj + 1].broadcast_to([128, 8]),
                                ALU.subtract),
                               wait=(("scalar", st["lden"]),) if jj == 0 else (),
                               inc="vector" if jj == 3 else None)

                    def r7():
                        st["w8e"] = op("scalar", lambda e: e.activation(
                            cols[:, W8E:W8E + 32], cols[:, LSH2:LSH2 + 32], AF.Exp),
                            wait=(("vector", st["lsh2"]),), inc="scalar")

                    def r8():
                        w = [("scalar", st["w8e"]), ("scr", cnt["scr"])]
                        wv = op("vector", lambda e: e.tensor_mul(
                            w8sb[:, wslot:wslot + 32], cols[:, W8E:W8E + 32],
                            cols[:, SEL:SEL + 32]),
                            wait=tuple(w), inc="vector")
                        sm = op("sync", dma(
                            w8_scr[blk * 512:(blk + 1) * 512, :].rearrange(
                                "(jj p) e -> p jj e", p=128),
                            w8sb[:, wslot:wslot + 32].rearrange(
                                "p (jj e) -> p jj e", e=8)),
                            wait=(("vector", wv),), inc="scr")
                        w8_store_ms[blk] = sm
                        setfree[blk % 2] = [wv, st["w8e"]]

                    r1()
                    return [r2, r3, r4, r5, r6, r7, r8]

                router_q.extend(mk_router())

                if a_stage < 3:
                    continue
                ps23_free = [glog_ready[NSUB - 1], l1_ms]
                for j in range(NSUB):
                    pump()
                    sbf_ms = s32bf_ready
                    gtw = [("vector", glog_ready[j])]
                    if prev_gla_done_s:
                        gtw.append(("scalar", prev_gla_done_s))
                    if prev_o_evac_v:
                        gtw.append(("vector", prev_o_evac_v))
                    gt_ms = op("tensor", lambda e, j=j: e.transpose(
                        ps[1][0:16, 256:384], glog[:, j * 16:(j + 1) * 16], idf[:]),
                        wait=tuple(gtw), inc="tensor")
                    if j == 3:
                        last_gt3_t = gt_ms
                    scw = [("tensor", gt_ms)]
                    if prev_gla_done_s:
                        scw.append(("scalar", prev_gla_done_s))
                    scan_ms = op("vector", lambda e: e.tensor_tensor_scan(
                        aT[:], ps[1][0:16, 256:384], ones16[:], 0.0,
                        ALU.add, ALU.bypass),
                        wait=tuple(scw), inc="vector")
                    alc_ms = op("scalar", lambda e: e.copy(
                        cols[0:16, 87:88], aT[:, 127:128]),
                       wait=(("vector", scan_ms),), inc="scalar")
                    op("vector", lambda e: e.tensor_scalar(
                        rT[:], aT[:], cols[0:16, 87:88], -1.0,
                        ALU.subtract, ALU.mult),
                       wait=(("scalar", alc_ms),))
                    lhsTt_ms = op("vector", lambda e: e.tensor_scalar_mul(
                        lhsTt[:], ones16[:], cols[0:16, 87:88]), inc="vector")
                    e2T_ms = op("scalar", lambda e: e.activation(e2T[:], rT[:], AF.Exp),
                                wait=(("vector", lhsTt_ms),), inc="scalar")
                    aw = [("vector", scan_ms)]
                    if prev_gla_done_s:
                        aw.append(("scalar", prev_gla_done_s))
                    atm_ms = op("tensor", lambda e: e.transpose(
                        ps[3][:, 384:400], aT[:], i16[:]),
                        wait=tuple(aw), inc="tensor")
                    ecolmm_ms = op("tensor", lambda e: e.matmul(
                        ps[3][:, 400:416], lhsTt[:], i16[:], start=True, stop=True),
                        wait=(("vector", lhsTt_ms),), inc="tensor")
                    atmc_ms = op("scalar", lambda e: e.copy(aTM[:], ps[3][:, 384:400]),
                                 wait=(("tensor", ecolmm_ms),), inc="scalar")
                    expa_ms = op("scalar", lambda e: e.activation(
                        expa[:], ps[3][:, 384:400], AF.Exp), inc="scalar")
                    ecol_ms = op("scalar", lambda e: e.activation(
                        ecol[:], ps[3][:, 400:416], AF.Exp),
                        wait=(("tensor", ecolmm_ms), ("vector", s32_ready)), inc="scalar")

                    stt_ms = [0] * H
                    pp_ms = [0] * H
                    o_mm_ms = [0] * H
                    for h in range(H):
                        bB = ps[0] if h % 2 == 0 else ps[2]
                        bS = ps[1] if h % 2 == 0 else ps[3]
                        par = (h % 2) * 64
                        fh = (h // 2) * 512 + j * 128
                        bw = [("vector", scan_ms)]
                        if h >= 2:
                            bw.append(("vector", stt_ms[h - 2]))
                        elif h == 0 and prev_gla_done_v:
                            bw.append(("vector", prev_gla_done_v))
                        elif h == 1:
                            bw.append(("vector", ps23_free[0]))
                            bw.append(("scalar", ps23_free[1]))
                            if prev_gla_done_v:
                                bw.append(("vector", prev_gla_done_v))
                        b_ms = op("tensor", lambda e, bB=bB, h=h: e.matmul(
                            bB[:, 0:128], i16[:, h:h + 1].broadcast_to([16, 128]),
                            aT[:], start=True, stop=True),
                            wait=tuple(bw), inc="tensor")
                        sw = [("vector", max(kg_ready, q_ready))]
                        if h >= 2:
                            sw.append(("vector", pp_ms[h - 2]))
                        elif h == 1:
                            sw.append(("scalar", ecol_ms))
                            sw.append(("scalar", expa_ms))
                            if j == 0 and blk in router_lv:
                                sw.append(("vector", router_lv[blk]))
                        sc_ms = op("tensor", lambda e, bS=bS, par=par, fh=fh: e.matmul(
                            bS[:, 0:128],
                            ab[par:par + 64, KGF + fh:KGF + fh + 128],
                            ab[par:par + 64, QFM + fh:QFM + fh + 128],
                            start=True, stop=True),
                            wait=tuple(sw), inc="tensor")
                        dsl = (h % 2) * 128
                        stt_ms[h] = op("vector", lambda e, bB=bB, h=h, dsl=dsl: e.scalar_tensor_tensor(
                            dmp[:, dsl:dsl + 128], bB[:, 0:128], aTM[:, h:h + 1],
                            mneg[:], ALU.subtract, ALU.add),
                            wait=(("tensor", b_ms), ("scalar", atmc_ms)), inc="vector")
                        gam_ms = op("scalar", lambda e, dsl=dsl: e.activation(
                            gam[:, dsl:dsl + 128], dmp[:, dsl:dsl + 128], AF.Exp),
                            wait=(("vector", stt_ms[h]),), inc="scalar")
                        pp_ms[h] = op("vector", lambda e, bS=bS, dsl=dsl: e.tensor_tensor(
                            gam[:, dsl:dsl + 128], bS[:, 0:128], gam[:, dsl:dsl + 128],
                            ALU.mult),
                            wait=(("scalar", gam_ms), ("tensor", sc_ms)), inc="vector")
                        obank = ps[4 + h // 8]
                        ibank = ps[6 + h // 8]
                        osl = (h % 8) * 64
                        ow = [("vector", pp_ms[h]), ("vector", v_ready)]
                        if h == 0:
                            if prev_o_evac_v:
                                ow.append(("vector", prev_o_evac_v))
                            ow.append(("vector", s32_ready))
                            if rms_sq_last_s:
                                ow.append(("scalar", rms_sq_last_s))
                        op("tensor", lambda e, obank=obank, osl=osl, dsl=dsl, j=j, h=h: e.matmul(
                            obank[:, osl:osl + 64], gam[:, dsl:dsl + 128],
                            ab[:, VTM + j * 1024 + h * 64:VTM + j * 1024 + (h + 1) * 64],
                            start=True, stop=True),
                            wait=tuple(ow))
                        iw = [("vector", sbf_ms)]
                        if h == 0:
                            iw.append(("vector", xnfm_ready))
                        o_mm_ms[h] = op("tensor", lambda e, ibank=ibank, osl=osl, par=par, fh=fh, h=h: e.matmul(
                            ibank[:, osl:osl + 64],
                            ab[par:par + 64, QFM + fh:QFM + fh + 128],
                            Sbf[par:par + 64, (h // 2) * 64:(h // 2) * 64 + 64],
                            start=True, stop=True),
                            wait=tuple(iw), inc="tensor")
                    prev_inter15_t = o_mm_ms[15]
                    pump()
                    o_evac_v = 0
                    ot_free = [0, 0]
                    for h in range(H):
                        obank = ps[4 + h // 8]
                        ibank = ps[6 + h // 8]
                        osl = (h % 8) * 64
                        osl2 = (h % 2) * 64
                        ew = []
                        if h == 0:
                            ew = [("tensor", o_mm_ms[15])]
                        if ot_free[h % 2]:
                            ew.append(("vector", ot_free[h % 2]))
                        ot_ms = op("scalar", lambda e, ibank=ibank, osl=osl, osl2=osl2, h=h: e.activation(
                            ot1[:, osl2:osl2 + 64], ibank[:, osl:osl + 64],
                            AF.Identity, scale=expa[:, h:h + 1]),
                           wait=tuple(ew), inc="scalar")
                        o_evac_v = op("vector", lambda e, obank=obank, osl=osl, osl2=osl2, j=j, h=h: e.tensor_tensor(
                            ab[:, YTM + j * 1024 + h * 64:YTM + j * 1024 + (h + 1) * 64],
                            ot1[:, osl2:osl2 + 64], obank[:, osl:osl + 64], ALU.add),
                           wait=(("scalar", ot_ms),), inc="vector")
                        ot_free[h % 2] = o_evac_v
                    ps0bf = ps[0][:].bitcast(BF16)
                    ktr_ms = 0
                    for c in range(8):
                        tw = [("vector", stt_ms[14])] if c == 0 else ()
                        ktr_ms = op("tensor", lambda e, c=c, j=j: e.transpose(
                            ps0bf[:, c * 128:(c + 1) * 128],
                            ab[:, KGF + c * 512 + j * 128:KGF + c * 512 + (j + 1) * 128],
                            idbf[:]),
                            wait=tuple(tw), inc="tensor" if c == 7 else None)
                    e2a_ms = op("tensor", lambda e: e.matmul(
                        ps[1][:, 0:512], e2T[:], exp1[:, 0:512], start=True, stop=True),
                        wait=(("scalar", e2T_ms), ("vector", pp_ms[14])), inc="tensor")
                    e2bw = [("vector", pp_ms[15])]
                    if j == 0 and blk in router_lv:
                        e2bw.append(("vector", router_lv[blk]))
                    e2b_ms = op("tensor", lambda e: e.matmul(
                        ps[3][:, 0:512], e2T[:], exp1[:, 512:1024], start=True, stop=True),
                        wait=tuple(e2bw), inc="tensor")
                    op("scalar", lambda e: e.copy(ab[:, E2X:E2X + 512], ps[1][:, 0:512]),
                       wait=(("tensor", e2a_ms),))
                    e2be_ms = op("scalar", lambda e: e.copy(
                        ab[:, E2X + 512:E2X + 1024], ps[3][:, 0:512]),
                        wait=(("tensor", e2b_ms),), inc="scalar")
                    kt_ms = op("vector", lambda e: e.tensor_tensor(
                        ab[:, KT:KT + 1024], ps0bf[:, 0:1024], ab[:, E2X:E2X + 1024],
                        ALU.mult),
                        wait=(("tensor", ktr_ms), ("scalar", e2be_ms)), inc="vector")
                    st_mm = [0] * H
                    for h in range(H):
                        par = (h % 2) * 64
                        fsl = (h // 2) * 64
                        sw2 = [("vector", kt_ms)]
                        if h == 0:
                            sw2.append(("vector", o_evac_v))
                        st_mm[h] = op("tensor", lambda e, h=h, j=j, par=par, fsl=fsl: e.matmul(
                            ps[4][par:par + 64, fsl:fsl + 64],
                            ab[:, KT + h * 64:KT + (h + 1) * 64],
                            ab[:, VTM + j * 1024 + h * 64:VTM + j * 1024 + (h + 1) * 64],
                            start=True, stop=True, tile_position=(0, par)),
                            wait=tuple(sw2), inc="tensor")
                    for h in range(H):
                        par = (h % 2) * 64
                        fsl = (h // 2) * 64
                        s32bf_ready = op("vector", lambda e, h=h, par=par, fsl=fsl: e.scalar_tensor_tensor(
                            Sbf[par:par + 64, fsl:fsl + 64],
                            S32[par:par + 64, fsl:fsl + 64],
                            ecol[par:par + 64, h:h + 1],
                            ps[4][par:par + 64, fsl:fsl + 64],
                            ALU.mult, ALU.add),
                            wait=(("tensor", st_mm[H - 1]), ("scalar", ecol_ms)),
                            inc="vector")
                        s32_ready = op("vector", lambda e, h=h, par=par, fsl=fsl: e.scalar_tensor_tensor(
                            S32[par:par + 64, fsl:fsl + 64],
                            S32[par:par + 64, fsl:fsl + 64],
                            ecol[par:par + 64, h:h + 1],
                            ps[4][par:par + 64, fsl:fsl + 64],
                            ALU.mult, ALU.add),
                            inc="vector")
                    prev_o_evac_v = o_evac_v
                    prev_gla_done_v = kt_ms
                    prev_gla_done_s = e2be_ms

                ps01_free = [[prev_gla_done_v, prev_gla_done_s],
                             [prev_gla_done_v, prev_gla_done_s]]
                ps23_free = [s32_ready, prev_gla_done_s]
                if dbg == "full":
                    op("sync", dma(
                        ydbg[T0:T0 + 512, :].rearrange("(jj p) n -> p jj n", p=128),
                        ab[:, YTM:YTM + 4096].rearrange("p (jj n) -> p jj n", n=DM)),
                        wait=(("vector", prev_o_evac_v),), inc="scr")
                    op("sync", dma(gdbg[blk, :, :], glog[:]),
                        wait=(("vector", glog_ready[NSUB - 1]),), inc="scr")

                if a_stage < 4:
                    continue
                pump()
                t1_ms = 0
                for j in range(NSUB):
                    ysl = ab[:, YTM + j * 1024:YTM + (j + 1) * 1024]
                    sqw = [("vector", prev_o_evac_v)]
                    if t1_ms:
                        sqw.append(("vector", t1_ms))
                    rms_sq_last_s = op("scalar", lambda e, ysl=ysl: e.activation(
                        ab[:, GSP:GSP + 1024], ysl, AF.Square),
                       wait=tuple(sqw), inc="scalar")
                    op("vector", lambda e: e.reduce_sum(
                        cols[:, 8:9], ab[:, GSP:GSP + 1024], axis=AX),
                       wait=(("scalar", rms_sq_last_s),))
                    red2_ms = op("vector", lambda e: e.reduce_sum(
                        cols[:, 79:80], cols[:, 8:10], axis=AX), inc="vector")
                    op("scalar", lambda e: e.activation(
                        cols[:, 9:10], cols[:, 8:9], AF.Ln, bias=epsb[:, 1:2],
                        scale=1.0 / DM),
                       wait=(("vector", red2_ms),))
                    rstd2_ms = op("scalar", lambda e: e.activation(
                        cols[:, 10:11], cols[:, 9:10], AF.Exp, scale=-0.5),
                        inc="scalar")
                    t1_ms = op("vector", lambda e, ysl=ysl, j=j: e.tensor_scalar_mul(
                        ab[:, XNT + (j % 2) * 1024:XNT + (j % 2 + 1) * 1024],
                        ysl, cols[:, 10:11]),
                       wait=(("scalar", rstd2_ms), ("tensor", xnt_free[j % 2])),
                       inc="vector")
                    last_yg_v = op("vector", lambda e, j=j: e.tensor_tensor(
                        ab[:, XNT + (j % 2) * 1024:XNT + (j % 2 + 1) * 1024],
                        ab[:, XNT + (j % 2) * 1024:XNT + (j % 2 + 1) * 1024],
                        ab[:, OGS + j * 1024:OGS + (j + 1) * 1024], ALU.mult),
                       wait=(("scalar", og_ready),), inc="vector")
                    tl = 0
                    for c in range(8):
                        trw = []
                        if c == 0:
                            trw = [("vector", last_yg_v), ("scr", cnt["scr"])]
                        tl = op("tensor", lambda e, c=c, j=j: e.transpose(
                            ps6bf[:, c * 128:(c + 1) * 128],
                            ab[:, XNT + (j % 2) * 1024 + c * 128:XNT + (j % 2) * 1024 + (c + 1) * 128],
                            idbf[:]),
                            wait=tuple(trw), inc="tensor" if c == 7 else None)
                    xnt_free[j % 2] = tl
                    dst = ab[:, XNFM:XNFM + 4096].rearrange(
                        "p (c u) -> p c u", u=512)[:, :, j * 128:(j + 1) * 128]
                    src = ps6bf[:].rearrange("p (c u) -> p c u", u=128)
                    last_ygf_v = op("vector",
                                    lambda e, dst=dst, src=src: e.tensor_copy(dst, src),
                                    wait=(("tensor", tl),), inc="vector")
                    for cc in range(2):
                        bank = grp % 2
                        for kk in range(8):
                            wv = []
                            if kk == 0:
                                wv = pswait(bank)
                                wv.append(("vector", last_ygf_v))
                                wv.append(("vector", wofold_ms))
                            mml = op("tensor", lambda e, bank=bank, j=j, cc=cc, kk=kk: e.matmul(
                                ps[bank][:],
                                ab[:, XNFM + kk * 512 + j * 128:XNFM + kk * 512 + (j + 1) * 128],
                                wa[:, WOFF["o"] + kk * DM + cc * 512:WOFF["o"] + kk * DM + (cc + 1) * 512],
                                start=(kk == 0), stop=(kk == 7)),
                                wait=tuple(wv), inc="tensor" if kk == 7 else None)
                        sev_ms = op("vector", lambda e, bank=bank, j=j, cc=cc: e.tensor_copy(
                            ab[:, SEV + (j % 2) * 1024 + cc * 512:SEV + (j % 2) * 1024 + (cc + 1) * 512],
                            ps[bank][:]),
                            wait=(("tensor", mml), ("scr", cnt["scr"])), inc="vector")
                        ps01_free[bank][0] = sev_ms
                        grp += 1
                    op("sync", dma(
                        s_scr[T0 + j * 128:T0 + (j + 1) * 128, :],
                        ab[:, SEV + (j % 2) * 1024:SEV + (j % 2 + 1) * 1024]),
                        wait=(("vector", sev_ms),), inc="scr")

        passA_scr = cnt["scr"]

        if "A" in passes and "B0" not in passes:
            while router_q:
                router_q.popleft()()
        BSEM = ("w0", "w1", "w2")
        for gi, grp_experts in enumerate(GROUPS):
            if f"B{gi}" not in passes:
                continue
            wB = []
            for idx, e_ in enumerate(grp_experts):
                aw = (("tensor", cnt["tensor"]),) if idx == 0 else ()
                op("gpsimd", dma(
                    wa[:, idx * 16384:idx * 16384 + 8192].rearrange("p (k n) -> p k n", n=DM),
                    w1T[e_].rearrange("(k p) n -> p k n", p=128)),
                    wait=aw, inc=BSEM[idx])
                m2 = op("gpsimd", dma(
                    wa[:, idx * 16384 + 8192:idx * 16384 + 16384].rearrange("p (k n) -> p k n", n=DM),
                    w2T[e_].rearrange("(k p) n -> p k n", p=128)), inc=BSEM[idx])
                wB.append(m2)
            pingv = [cnt["vector"], cnt["vector"]]
            pings = [cnt["scalar"], cnt["scalar"]]
            pingd = [cnt["vector"], cnt["vector"]]
            rfree = [0, 0]
            prev_store = 0
            ydone_ms = 0
            for blk in range(NBLK):
                T0 = blk * 512
                if gi == 0 and router_q:
                    router_q.popleft()()
                axn_ms = op("sync", dma(
                    ab[:, XNFM:XNFM + 4096].rearrange("p (c u) -> p c u", u=512),
                    xnfm_scr[:, :, T0:T0 + 512].rearrange("c p u -> p c u")),
                    wait=(("scr", passA_scr), ("tensor", cnt["tensor"])), inc="axn")
                w8lw = [("vector", cnt["vector"])]
                if blk in w8_store_ms:
                    w8lw.append(("scr", w8_store_ms[blk]))
                w8l_ms = op("sync", dma(
                    w8sb[:, 0:32].rearrange("p (jj e) -> p jj e", e=8),
                    w8_scr[T0:T0 + 512, :].rearrange("(jj p) e -> p jj e", p=128)),
                    wait=tuple(w8lw), inc="w8l")
                ay_ms = 0
                if gi > 0:
                    ay_ms = op("sync", dma(
                        ab[:, Y0T:Y0T + 4096].rearrange("p (jj n) -> p jj n", n=DM),
                        y_scr[gi - 1][T0:T0 + 512, :].rearrange("(jj p) n -> p jj n", p=128)),
                        wait=(("yc", cnt["yc"]), ("vector", cnt["vector"])), inc="ayl")
                for ei, e_ in enumerate(grp_experts):
                    h2_ms = 0
                    for fc in range(8):
                        bank = fc % 2
                        for kk in range(8):
                            wv = []
                            if kk == 0:
                                wv.append(("axn", axn_ms))
                                wv.append((BSEM[ei], wB[ei]))
                                if pingv[bank]:
                                    wv.append(("vector", pingv[bank]))
                                if pings[bank]:
                                    wv.append(("scalar", pings[bank]))
                            mml = op("tensor", lambda e, bank=bank, ei=ei, fc=fc, kk=kk: e.matmul(
                                ps[bank][:],
                                wa[:, ei * 16384 + kk * DM + fc * 128:ei * 16384 + kk * DM + (fc + 1) * 128],
                                ab[:, XNFM + kk * 512:XNFM + (kk + 1) * 512],
                                start=(kk == 0), stop=(kk == 7)),
                                wait=tuple(wv), inc="tensor" if kk == 7 else None)
                        rsl = RPG + (fc % 2) * 512
                        rw = [("tensor", mml)]
                        if rfree[fc % 2]:
                            rw.append(("vector", rfree[fc % 2]))
                        r_ms = op("scalar", lambda e, bank=bank, rsl=rsl: e.activation(
                            ab[:, rsl:rsl + 512], ps[bank][:], AF.Relu),
                            wait=tuple(rw), inc="scalar")
                        pings[bank] = r_ms
                        h2_ms = op("vector", lambda e, fc=fc, rsl=rsl: e.tensor_tensor(
                            ab[:, H2 + fc * 512:H2 + (fc + 1) * 512],
                            ab[:, rsl:rsl + 512], ab[:, rsl:rsl + 512], ALU.mult),
                            wait=(("scalar", r_ms),), inc="vector")
                        rfree[fc % 2] = h2_ms
                        pingv[bank] = h2_ms
                    for j in range(NSUB):
                        for cc in range(2):
                            bank = 2 + ((j * 2 + cc) % 2)
                            for kk in range(8):
                                wv = []
                                if kk == 0:
                                    wv.append(("vector", h2_ms))
                                    if pingd[bank - 2]:
                                        wv.append(("vector", pingd[bank - 2]))
                                mml = op("tensor", lambda e, bank=bank, ei=ei, j=j, cc=cc, kk=kk: e.matmul(
                                    ps[bank][:],
                                    ab[:, H2 + kk * 512 + j * 128:H2 + kk * 512 + (j + 1) * 128],
                                    wa[:, ei * 16384 + 8192 + kk * DM + cc * 512:ei * 16384 + 8192 + kk * DM + (cc + 1) * 512],
                                    start=(kk == 0), stop=(kk == 7)),
                                    wait=tuple(wv), inc="tensor" if kk == 7 else None)
                            dw = [("tensor", mml), ("w8l", w8l_ms)]
                            if ei == 0:
                                if prev_store:
                                    dw.append(("yc", prev_store))
                                if gi > 0:
                                    dw.append(("ayl", ay_ms))
                            wcol = w8sb[:, j * 8 + e_:j * 8 + e_ + 1]
                            ysl = af[:, j * 1024 + cc * 512:j * 1024 + (cc + 1) * 512]
                            if ei == 0 and gi == 0:
                                fn = (lambda e, bank=bank, wcol=wcol, ysl=ysl:
                                      e.tensor_scalar_mul(ysl, ps[bank][:], wcol))
                            elif ei == 0:
                                y0sl = ab[:, Y0T + j * 1024 + cc * 512:Y0T + j * 1024 + (cc + 1) * 512]
                                fn = (lambda e, bank=bank, wcol=wcol, ysl=ysl, y0sl=y0sl:
                                      e.scalar_tensor_tensor(ysl, ps[bank][:], wcol,
                                                             y0sl, ALU.mult, ALU.add))
                            else:
                                fn = (lambda e, bank=bank, wcol=wcol, ysl=ysl:
                                      e.scalar_tensor_tensor(ysl, ps[bank][:], wcol,
                                                             ysl, ALU.mult, ALU.add))
                            ydone_ms = op("vector", fn, wait=tuple(dw), inc="vector")
                            pingd[bank - 2] = ydone_ms
                prev_store = op("gpsimd", dma(
                    y_scr[gi][T0:T0 + 512, :].rearrange("(jj p) n -> p jj n", p=128),
                    af[:, 0:4096].rearrange("p (jj n) -> p jj n", n=DM)),
                    wait=(("vector", ydone_ms),), inc="yc")

        if "C" in passes:
            m1 = op("gpsimd", dma(
                wa[:, 0:16384].rearrange("p (k n) -> p k n", n=2048),
                ws1T.rearrange("(k p) n -> p k n", p=128)),
                wait=(("tensor", cnt["tensor"]),), inc="w3")
            m2 = op("gpsimd", dma(
                wa[:, 16384:32768].rearrange("p (k n) -> p k n", n=DM),
                ws2T.rearrange("(k p) n -> p k n", p=128)), inc="w4")
            yc_total = cnt["yc"]
            pingv = [cnt["vector"], cnt["vector"]]
            pings = [cnt["scalar"], cnt["scalar"]]
            pingd = [cnt["vector"], cnt["vector"]]
            rfree = [0, 0]
            prev_out_v = 0
            for blk in range(NBLK):
                T0 = blk * 512
                axn_ms = op("sync", dma(
                    ab[:, XNFM:XNFM + 4096].rearrange("p (c u) -> p c u", u=512),
                    xnfm_scr[:, :, T0:T0 + 512].rearrange("c p u -> p c u")),
                    wait=(("scr", passA_scr), ("tensor", cnt["tensor"])), inc="axn")
                y2_ms = op("sync", dma(
                    ab[:, Y2T:Y2T + 4096].rearrange("p (jj n) -> p jj n", n=DM),
                    y_scr[2][T0:T0 + 512, :].rearrange("(jj p) n -> p jj n", p=128)),
                    wait=(("yc", yc_total), ("vector", cnt["vector"])), inc="ayl")
                s_ms = op("sync", dma(
                    ab[:, SBLK:SBLK + 4096].rearrange("p (jj n) -> p jj n", n=DM),
                    s_scr[T0:T0 + 512, :].rearrange("(jj p) n -> p jj n", p=128)),
                    wait=(("vector", cnt["vector"]),), inc="asl")
                xbw = [("scr", cnt["scr"])]
                if prev_out_v:
                    xbw.append(("vector", prev_out_v))
                xb_ms = op("sync", dma(
                    af[:, 0:4096].rearrange("p (jj n) -> p jj n", n=DM),
                    x_h[T0:T0 + 512, :].rearrange("(jj p) n -> p jj n", p=128)),
                    wait=tuple(xbw), inc="xl0")
                h2_ms = 0
                for fc in range(16):
                    bank = fc % 2
                    for kk in range(8):
                        wv = []
                        if kk == 0:
                            wv.append(("axn", axn_ms))
                            wv.append(("w3", m1))
                            if pingv[bank]:
                                wv.append(("vector", pingv[bank]))
                            if pings[bank]:
                                wv.append(("scalar", pings[bank]))
                        mml = op("tensor", lambda e, bank=bank, fc=fc, kk=kk: e.matmul(
                            ps[bank][:],
                            wa[:, kk * 2048 + fc * 128:kk * 2048 + (fc + 1) * 128],
                            ab[:, XNFM + kk * 512:XNFM + (kk + 1) * 512],
                            start=(kk == 0), stop=(kk == 7)),
                            wait=tuple(wv), inc="tensor" if kk == 7 else None)
                    rsl = RPGC + (fc % 2) * 512
                    rw = [("tensor", mml)]
                    if rfree[fc % 2]:
                        rw.append(("vector", rfree[fc % 2]))
                    r_ms = op("scalar", lambda e, bank=bank, rsl=rsl: e.activation(
                        ab[:, rsl:rsl + 512], ps[bank][:], AF.Relu),
                        wait=tuple(rw), inc="scalar")
                    pings[bank] = r_ms
                    h2_ms = op("vector", lambda e, fc=fc, rsl=rsl: e.tensor_tensor(
                        ab[:, H2C + fc * 512:H2C + (fc + 1) * 512],
                        ab[:, rsl:rsl + 512], ab[:, rsl:rsl + 512], ALU.mult),
                        wait=(("scalar", r_ms),), inc="vector")
                    rfree[fc % 2] = h2_ms
                    pingv[bank] = h2_ms
                for j in range(NSUB):
                    for cc in range(2):
                        bank = 2 + ((j * 2 + cc) % 2)
                        for kk in range(16):
                            wv = []
                            if kk == 0:
                                wv.append(("vector", h2_ms))
                                wv.append(("w4", m2))
                                if pingd[bank - 2]:
                                    wv.append(("vector", pingd[bank - 2]))
                            mml = op("tensor", lambda e, bank=bank, j=j, cc=cc, kk=kk: e.matmul(
                                ps[bank][:],
                                ab[:, H2C + kk * 512 + j * 128:H2C + kk * 512 + (j + 1) * 128],
                                wa[:, 16384 + kk * DM + cc * 512:16384 + kk * DM + (cc + 1) * 512],
                                start=(kk == 0), stop=(kk == 15)),
                                wait=tuple(wv), inc="tensor" if kk == 15 else None)
                        gsl = (j * 2 + cc) % 2
                        gin_ms = op("vector", lambda e, bank=bank, j=j, cc=cc, gsl=gsl: e.tensor_tensor(
                            gin[:, gsl * 512:(gsl + 1) * 512], ps[bank][:],
                            ab[:, Y2T + j * 1024 + cc * 512:Y2T + j * 1024 + (cc + 1) * 512],
                            ALU.add),
                            wait=(("tensor", mml), ("ayl", y2_ms)), inc="vector")
                        sil_ms = op("scalar", lambda e, gsl=gsl: e.activation(
                            ab[:, RPGC + gsl * 512:RPGC + (gsl + 1) * 512],
                            gin[:, gsl * 512:(gsl + 1) * 512], AF.Silu),
                            wait=(("vector", gin_ms),), inc="scalar")
                        sg_ms = op("vector", lambda e, j=j, cc=cc, gsl=gsl: e.tensor_tensor(
                            sgb[:, gsl * 512:(gsl + 1) * 512],
                            ab[:, SBLK + j * 1024 + cc * 512:SBLK + j * 1024 + (cc + 1) * 512],
                            ab[:, RPGC + gsl * 512:RPGC + (gsl + 1) * 512], ALU.mult),
                            wait=(("scalar", sil_ms), ("asl", s_ms)), inc="vector")
                        pingd[bank - 2] = sg_ms
                        prev_out_v = op("vector", lambda e, j=j, cc=cc, gsl=gsl: e.tensor_tensor(
                            af[:, j * 1024 + cc * 512:j * 1024 + (cc + 1) * 512],
                            af[:, j * 1024 + cc * 512:j * 1024 + (cc + 1) * 512],
                            sgb[:, gsl * 512:(gsl + 1) * 512], ALU.add),
                            wait=(("xl0", xb_ms),), inc="vector")
                op("sync", dma(
                    out_h[T0:T0 + 512, :].rearrange("(jj p) n -> p jj n", p=128),
                    af[:, 0:4096].rearrange("p (jj n) -> p jj n", n=DM)),
                    wait=(("vector", prev_out_v),), inc="scr")

        final_scr = cnt["scr"]
        final_yc = cnt["yc"]
        if final_yc:
            op("sync", lambda e: e.wait_ge(sems["yc"], final_yc))
        op("sync", lambda e: e.wait_ge(sems["scr"], final_scr))

        globals()["_LAST_PROG"] = P
        def runner(name):
            def run(eng):
                last = {}
                for fn, waits, inc in P.q[name]:
                    for sname, val in waits:
                        if not val:
                            continue
                        if last.get(sname, 0) < val:
                            eng.wait_ge(sems[sname], val)
                            last[sname] = val
                    ins = fn(eng)
                    if inc is not None:
                        amt = 16 if inc in DMA_SEMS else 1
                        ins.then_inc(sems[inc], amt)
            return run

        block.tensor(runner("tensor"))
        block.vector(runner("vector"))
        block.scalar(runner("scalar"))
        block.sync(runner("sync"))
        block.gpsimd(runner("gpsimd"))

    return nc


def _prep_shared(inputs):
    f = np.float32
    c = np.ascontiguousarray
    d = {}
    d["wqT"] = c(np.asarray(inputs["L0_attn_Wq"], f).T)
    d["wkT"] = c(np.asarray(inputs["L0_attn_Wk"], f).T)
    d["wgT"] = c(np.asarray(inputs["L0_attn_Wg"], f).T)
    d["wvT"] = c(np.asarray(inputs["L0_attn_Wv"], f).T)
    d["wogT"] = c(np.asarray(inputs["L0_attn_Wog"], f).T)
    d["woT"] = c(np.asarray(inputs["L0_attn_Wo"], f).T)
    d["wdT"] = c(np.asarray(inputs["L0_attn_Wd"], f).T)
    d["wrT"] = c(np.asarray(inputs["L0_ffn_w_router"], f).T)
    d["w1T"] = c(np.asarray(inputs["L0_ffn_w1_stack"], f).transpose(0, 2, 1))
    d["w2T"] = c(np.asarray(inputs["L0_ffn_w2_stack"], f).transpose(0, 2, 1))
    d["ws1T"] = c(np.asarray(inputs["L0_ffn_ws1"], f).T)
    d["ws2T"] = c(np.asarray(inputs["L0_ffn_ws2"], f).T)
    d["bdr"] = c(np.asarray(inputs["L0_attn_bd"], f).reshape(1, H))
    d["nwc"] = c(np.asarray(inputs["L0_attn_norm_w"], f).reshape(8, 128).T)
    d["idbf"] = np.eye(128, dtype=ml_dtypes.bfloat16)
    d["idf"] = np.eye(128, dtype=f)
    d["i16"] = np.eye(16, dtype=f)
    d["ones16"] = np.ones((16, 128), f)
    d["ones1"] = np.ones((1, 128), f)
    tau = np.arange(128)[:, None]
    ii = np.arange(128)[None, :]
    d["mneg"] = np.where(ii >= tau, 0.0, -1e30).astype(f)
    ex = np.zeros((16, DM), f)
    for h_ in range(16):
        ex[h_, h_ * 64:(h_ + 1) * 64] = 1.0
    d["exp1"] = ex.astype(ml_dtypes.bfloat16)
    d["epsb"] = np.tile(np.array([[EPS_LN, EPS_RMS]], f), (128, 1))
    return d


def kernel(**inputs):
    x = np.ascontiguousarray(np.asarray(inputs["L0_attn_x"], np.float32))
    shared = _prep_shared(inputs)
    nc = build_nc()
    in_maps = []
    for cid in range(8):
        m = dict(shared)
        m["x"] = x[cid]
        in_maps.append(m)
    res = run_bass_kernel_spmd(nc, in_maps, core_ids=list(range(8)))
    return np.stack([res.results[cid]["out"] for cid in range(8)], axis=0)
